# version 04
# baseline (speedup 1.0000x reference)
import math
import numpy as np
import concourse.bass as bass
import concourse.mybir as mybir
from concourse.bass_utils import run_bass_kernel_spmd

F32 = mybir.dt.float32
BF16 = mybir.dt.bfloat16
AF = mybir.ActivationFunctionType
ALU = mybir.AluOpType
AX = mybir.AxisListType

NCORE = 8
ENGS = ("tensor", "vector", "scalar", "gpsimd", "sync")


class Emit:

    def __init__(self):
        self.q = {e: [] for e in ENGS}
        self.cnt = {}

    def on(self, eng, fn):
        self.q[eng].append(fn)

    def wait(self, eng, sem, val):
        if val is None or val <= 0:
            return
        self.on(eng, lambda e, s=sem, v=val: e.wait_ge(s, v))

    def dma(self, eng, sem, name, out, in_):
        self.cnt[name] = self.cnt.get(name, 0) + 16
        v = self.cnt[name]
        self.on(eng, lambda e, o=out, i=in_, s=sem:
                e.dma_start(out=o, in_=i).then_inc(s, 16))
        return v

    def op(self, eng, fn, sem=None, name=None):
        if sem is not None:
            self.cnt[name] = self.cnt.get(name, 0) + 1
            v = self.cnt[name]
            self.on(eng, lambda e, f=fn, s=sem: f(e).then_inc(s, 1))
            return v
        self.on(eng, lambda e, f=fn: f(e))
        return None

    def barrier(self, bar, selfw=()):
        for eng, semh, nm in selfw:
            self.wait(eng, semh, self.cnt.get(nm, 0))
        self.cnt["bar"] = self.cnt.get("bar", 0) + 5
        v = self.cnt["bar"]
        for e in ENGS:
            self.on(e, lambda en, s=bar: en.nop().then_inc(s, 1))
        for e in ENGS:
            self.wait(e, bar, v)
        return v


def _host_prep(inputs):
    x = np.ascontiguousarray(np.asarray(inputs["L0_attn_x"], dtype=np.float32))
    w1 = np.asarray(inputs["L0_attn_w1"], dtype=np.float32)
    w2 = np.asarray(inputs["L0_attn_w2"], dtype=np.float32)
    b2 = np.asarray(inputs["L0_attn_b2"], dtype=np.float32)
    mask = np.asarray(inputs["L0_attn_sparse_mask"], dtype=np.float32)
    W_org = np.asarray(inputs["L0_ffn_W_org"], dtype=np.float32)
    dn = np.asarray(inputs["L0_ffn_lora_down"], dtype=np.float32)
    up = np.asarray(inputs["L0_ffn_lora_up"], dtype=np.float32)
    keys = np.asarray(inputs["L0_ffn_lora_keys"], dtype=np.float32)
    Wq = np.asarray(inputs["L0_ffn_Wq"], dtype=np.float32)

    Bm = mask.reshape(32, 128, 32, 128).any(axis=(1, 3))
    scl = []
    for r in range(NCORE):
        lst = [sc for sc in range(32) if Bm[sc, 4 * r:4 * r + 4].any()]
        if not lst:
            lst = [0]
        scl.append(lst)
    n_sc = max(len(l) for l in scl)

    shared = {}
    shared["wtt"] = np.ascontiguousarray(
        W_org.T.reshape(32, 128, 32, 128).transpose(2, 1, 0, 3))
    D = dn.transpose(1, 0, 2).reshape(4096, 512)
    shared["dnt"] = np.ascontiguousarray(
        D.reshape(32, 128, 4, 128).transpose(2, 1, 0, 3))
    U = up.reshape(512, 4096)
    shared["upt"] = np.ascontiguousarray(
        U.reshape(4, 128, 32, 128).transpose(2, 1, 0, 3))
    shared["wqt"] = np.ascontiguousarray(
        Wq.T.reshape(32, 128, 128).transpose(1, 0, 2))
    shared["keyst"] = np.ascontiguousarray(keys.T)
    shared["b2c"] = np.ascontiguousarray(b2.reshape(32, 128).T)
    shared["idn"] = np.eye(128, dtype=np.float32)
    shared["onesc"] = np.ones((128, 1), dtype=np.float32)
    shared["ones1f"] = np.ones((1, 128), dtype=np.float32)
    shared["onebf"] = np.ones((1, 1), dtype=np.float32)
    shared["epsc"] = np.full((1, 1), 1e-5, dtype=np.float32)
    sel8 = np.zeros((8, 512), dtype=np.float32)
    for j in range(512):
        sel8[j // 64, j] = 1.0
    shared["sel8"] = sel8

    W1R = w1.reshape(32, 128, 32, 128).transpose(2, 1, 0, 3)
    W2T = np.ascontiguousarray(w2.T.reshape(32, 128, 4096))

    in_maps = []
    for c in range(NCORE):
        b, h = c // 2, c % 2
        t0 = 512 * c
        lst = scl[c]
        w1sl = np.empty((n_sc, 128, 32, 128), dtype=np.float32)
        msl = np.zeros((n_sc, 128, 512), dtype=np.float32)
        for j in range(n_sc):
            sc = lst[j] if j < len(lst) else lst[0]
            w1sl[j] = W1R[sc]
            if j < len(lst):
                msl[j] = mask[sc * 128:(sc + 1) * 128, t0:t0 + 512]
        bsel = np.zeros((128, 4), dtype=np.float32)
        bsel[:, b] = 1.0
        m = dict(shared)
        m["xtm"] = np.ascontiguousarray(x[b, h * 1024:(h + 1) * 1024, :])
        m["w1sl"] = w1sl
        m["masksl"] = msl
        m["w2ts"] = np.ascontiguousarray(W2T[:, :, t0:t0 + 512])
        m["bselc"] = bsel
        in_maps.append(m)

    SCL = [[scl[r][j] if j < len(scl[r]) else scl[r][0] for j in range(n_sc)]
           for r in range(NCORE)]
    return in_maps, SCL, n_sc


def build(SCL, n_sc, stage=3, dbg=False):
    nc = bass.Bass("TRN2", target_bir_lowering=False)

    d_xtm = nc.dram_tensor("xtm", [1024, 4096], F32, kind="ExternalInput")
    d_wtt = nc.dram_tensor("wtt", [32, 128, 32, 128], F32, kind="ExternalInput")
    d_dnt = nc.dram_tensor("dnt", [4, 128, 32, 128], F32, kind="ExternalInput")
    d_upt = nc.dram_tensor("upt", [32, 128, 4, 128], F32, kind="ExternalInput")
    d_wqt = nc.dram_tensor("wqt", [128, 32, 128], F32, kind="ExternalInput")
    d_keyst = nc.dram_tensor("keyst", [128, 8], F32, kind="ExternalInput")
    d_b2c = nc.dram_tensor("b2c", [128, 32], F32, kind="ExternalInput")
    d_idn = nc.dram_tensor("idn", [128, 128], F32, kind="ExternalInput")
    d_onesc = nc.dram_tensor("onesc", [128, 1], F32, kind="ExternalInput")
    d_ones1f = nc.dram_tensor("ones1f", [1, 128], F32, kind="ExternalInput")
    d_onebf = nc.dram_tensor("onebf", [1, 1], F32, kind="ExternalInput")
    d_epsc = nc.dram_tensor("epsc", [1, 1], F32, kind="ExternalInput")
    d_sel8 = nc.dram_tensor("sel8", [8, 512], F32, kind="ExternalInput")
    d_w1sl = nc.dram_tensor("w1sl", [n_sc, 128, 32, 128], F32,
                            kind="ExternalInput")
    d_masksl = nc.dram_tensor("masksl", [n_sc, 128, 512], F32,
                              kind="ExternalInput")
    d_w2ts = nc.dram_tensor("w2ts", [32, 128, 512], F32, kind="ExternalInput")
    d_bselc = nc.dram_tensor("bselc", [128, 4], F32, kind="ExternalInput")
    d_out = nc.dram_tensor("out", [1024, 4096], F32, kind="ExternalOutput")
    d_dbg = nc.dram_tensor("dbg", [128, 128], F32, kind="ExternalOutput")
    d_dbg2 = nc.dram_tensor("dbg2", [128, 4096], F32, kind="ExternalOutput")
    d_dbg3 = nc.dram_tensor("dbg3", [32, 128, 1024], BF16, kind="ExternalOutput")

    d_x = nc.dram_tensor("x_home", [32, 128, 1024], F32)
    d_agin = nc.dram_tensor("ag_in", [n_sc * 128, 512], BF16)
    d_agout = nc.dram_tensor("ag_out", [NCORE * n_sc * 128, 512], BF16,
                             addr_space="Shared")
    d_ar1i = nc.dram_tensor("ar1_in", [512, 32], F32)
    d_ar1o = nc.dram_tensor("ar1_out", [512, 32], F32, addr_space="Shared")
    d_ar2i = nc.dram_tensor("ar2_in", [512, 32], F32)
    d_ar2o = nc.dram_tensor("ar2_out", [512, 32], F32, addr_space="Shared")

    import contextlib
    with contextlib.ExitStack() as st:
        def sb(name, shape, dt):
            return st.enter_context(nc.sbuf_tensor(name, shape, dt))

        def sem(name):
            return st.enter_context(nc.semaphore(name))

        ln = sb("ln", [128, 32768], BF16)
        w2sb = sb("w2sb", [128, 16384], BF16)
        wpan = [sb(f"wpan{i}", [128, 4096], BF16) for i in range(2)]
        w1pan = [sb(f"w1pan{i}", [128, 4096], BF16) for i in range(2)]
        uppan = [sb(f"uppan{i}", [128, 512], BF16) for i in range(2)]
        agb = uppan
        xinb = [sb(f"xinb{i}", [128, 1024], F32) for i in range(2)]
        mskb = [xinb[0][:, 0:512], xinb[1][:, 0:512]]
        asm = [sb(f"asm{i}", [128, 1024], F32) for i in range(2)]
        xbuf = [sb(f"xbuf{i}", [128, 1024], F32) for i in range(2)]
        tmp = [sb(f"tmp{i}", [128, 1024], F32) for i in range(2)]
        xbb = [sb(f"xbb{i}", [128, 1024], BF16) for i in range(2)]
        sqb = [sb(f"sqb{i}", [128, 1024], BF16) for i in range(2)]
        hb = sb("hb", [128, 4096], BF16)
        petmp = [sb(f"petmp{i}", [128, 512], F32) for i in range(2)]
        gtmp = [sb(f"gtmp{i}", [128, 1024], F32) for i in range(2)]
        gbuf = [tmp[0], tmp[1], gtmp[0], gtmp[1]]
        statrow = sb("statrow", [1, 3072], F32)
        t1 = tmp[0]
        pmsb = sb("pmsb", [128, 32], F32)
        pmz = sb("pmz", [128, 128], F32)
        armall = sb("armall", [128, 128], F32)
        macc = sb("macc", [128, 32], F32)
        meanb = sb("meanb", [128, 32], BF16)
        qb = sb("qb", [128, 1], BF16)
        srow = sb("srow", [1, 64], F32)
        crowb = sb("crowb", [1, 8], BF16)
        ccolb = sb("ccolb", [8, 1], BF16)
        coeffv = sb("coeffv", [128, 4], F32)
        b2sb = sb("b2sb", [128, 32], F32)
        bselb = sb("bselb", [128, 4], F32)
        idn_sb = sb("idn_sb", [128, 128], F32)
        onesb = sb("onesb", [128, 1], BF16)
        ones1f_sb = sb("ones1f_sb", [1, 128], F32)
        onebb = sb("onebb", [1, 1], BF16)
        selb = sb("selb", [8, 512], BF16)
        keysb = sb("keysb", [128, 8], BF16)
        epsb = sb("epsb", [1, 1], F32)

        pt = st.enter_context(nc.psum_tensor("pt", [128, 4096], F32))

        def bank(i, lo=0, n=512, p0=0, pn=128):
            return pt[p0:p0 + pn, i * 512 + lo:i * 512 + lo + n]

        sY_h = sem("sY")
        sP_h = sem("sP")
        sT_h = sem("sT")
        sS_h = sem("sS")
        sV_h = sem("sV")
        bar_h = sem("bar")
        sAG = sem("sAG")
        sAR1 = sem("sAR1")
        sAR2 = sem("sAR2")

        E = Emit()

        def BAR():
            return E.barrier(bar_h, selfw=(
                ("vector", sV_h, "sV"), ("scalar", sS_h, "sS"),
                ("tensor", sT_h, "sT")))

        def Y(out, in_):
            return E.dma("sync", sY_h, "sY", out, in_)

        def P(out, in_):
            return E.dma("gpsimd", sP_h, "sP", out, in_)

        def T(fn, inc=False):
            return E.op("tensor", fn, sem=sT_h if inc else None, name="sT")

        def S(fn, inc=False):
            return E.op("scalar", fn, sem=sS_h if inc else None, name="sS")

        def V(fn, inc=False):
            return E.op("vector", fn, sem=sV_h if inc else None, name="sV")

        def VC(fn):
            E.wait("vector", sV_h, E.cnt.get("sV", 0))
            return V(fn, inc=True)

        def SC(fn):
            E.wait("scalar", sS_h, E.cnt.get("sS", 0))
            return S(fn, inc=True)

        for di, so in [(d_onesc[:], onesb[:]), (d_onebf[:], onebb[:]),
                       (d_sel8[:], selb[:]), (d_keyst[:], keysb[:])]:
            P(so, di)
        for di, so in [(d_idn[:], idn_sb[:]), (d_ones1f[:], ones1f_sb[:]),
                       (d_b2c[:], b2sb[:]), (d_bselc[:], bselb[:]),
                       (d_epsc[:], epsb[:])]:
            Y(so, di)
        cY_const = E.cnt["sY"]
        for dc in range(32):
            P(w2sb[:, dc * 512:(dc + 1) * 512], d_w2ts[dc])
        cP_w2 = E.cnt["sP"]

        E.wait("tensor", sY_h, cY_const)
        xdr = d_x.rearrange("fc p t -> p fc t")
        cY_xin = {}
        cS_asm = {}
        cV_lnp = {}
        cY_asmw = {}
        cT_tp = {}
        for tb in range(8):
            for fq in range(4):
                k = tb * 4 + fq
                par = k % 2
                if k >= 2:
                    E.wait("sync", sT_h, cT_tp[k - 2])
                cY_xin[k] = Y(xinb[par][:],
                              d_xtm[tb * 128:(tb + 1) * 128,
                                    fq * 1024:(fq + 1) * 1024])
                E.wait("tensor", sY_h, cY_xin[k])
                if k >= 2:
                    E.wait("tensor", sY_h, cY_asmw[k - 2])
                    E.wait("tensor", sV_h, cV_lnp[k - 2])
                E.wait("tensor", sS_h, cS_asm.get(k - 1))
                for i in range(8):
                    tv = T(lambda e, par=par, i=i: e.transpose(
                        bank(i, 0, 128), xinb[par][:, i * 128:(i + 1) * 128],
                        idn_sb[:]), inc=(i == 7))
                    if i == 7:
                        cT_tp[k] = tv
                E.wait("scalar", sT_h, cT_tp[k])
                sv = None
                for i in range(8):
                    sv = S(lambda e, par=par, i=i: e.copy(
                        asm[par][:, i * 128:(i + 1) * 128], bank(i, 0, 128)),
                        inc=(i == 7))
                cS_asm[k] = sv
                E.wait("sync", sS_h, cS_asm[k])
                cY_asmw[k] = Y(
                    xdr[:, fq * 8:(fq + 1) * 8, tb * 128:(tb + 1) * 128],
                    asm[par].rearrange("p (fc t) -> p fc t", fc=8))
                E.wait("vector", sS_h, cS_asm[k])
                v = None
                for i in range(8):
                    fc = fq * 8 + i
                    v = V(lambda e, par=par, i=i, fc=fc, tb=tb: e.tensor_copy(
                        ln[:, fc * 1024 + tb * 128:fc * 1024 + (tb + 1) * 128],
                        asm[par][:, i * 128:(i + 1) * 128]), inc=(i == 7))
                cV_lnp[k] = v
        xw0 = {fc: cY_asmw[28 + fc // 8] for fc in range(32)}
        BAR()

        def ln_stats_from_ln():
            cT_st = {}
            for fc in range(32):
                E.wait("scalar", sT_h, cT_st.get(fc - 2))
                cs = S(lambda e, fc=fc: e.activation(
                    sqb[fc % 2][:], ln[:, fc * 1024:(fc + 1) * 1024],
                    AF.Square), inc=True)
                E.wait("tensor", sS_h, cs)
                for hh in range(2):
                    T(lambda e, fc=fc, hh=hh: e.matmul(
                        bank(4 + hh, 0, 512, 0, 1), onesb[:],
                        ln[:, fc * 1024 + hh * 512:fc * 1024 + hh * 512 + 512],
                        start=(fc == 0), stop=(fc == 31)))
                    tv = T(lambda e, fc=fc, hh=hh: e.matmul(
                        bank(6 + hh, 0, 512, 0, 1), onesb[:],
                        sqb[fc % 2][:, hh * 512:hh * 512 + 512],
                        start=(fc == 0), stop=(fc == 31)), inc=True)
                    cT_st[fc] = tv
            return cT_st[31]

        def ln_stats_from_x(xw):
            cT_st = {}
            cV_done = {}
            for fc in range(32):
                par = fc % 2
                E.wait("sync", sY_h, xw.get(fc) if xw else None)
                if fc >= 2:
                    E.wait("sync", sS_h, cV_done[fc - 2])
                cy = Y(xbuf[par][:], d_x[fc])
                E.wait("scalar", sY_h, cy)
                E.wait("scalar", sT_h, cT_st.get(fc - 2))
                SC(lambda e, par=par: e.copy(xbb[par][:], xbuf[par][:]))
                cs = SC(lambda e, par=par: e.activation(
                    sqb[par][:], xbb[par][:], AF.Square))
                cV_done[fc] = cs
                E.wait("tensor", sS_h, cs)
                for hh in range(2):
                    T(lambda e, par=par, hh=hh: e.matmul(
                        bank(4 + hh, 0, 512, 0, 1), onesb[:],
                        xbb[par][:, hh * 512:hh * 512 + 512],
                        start=(fc == 0), stop=(fc == 31)))
                    tv = T(lambda e, par=par, hh=hh: e.matmul(
                        bank(6 + hh, 0, 512, 0, 1), onesb[:],
                        sqb[par][:, hh * 512:hh * 512 + 512],
                        start=(fc == 0), stop=(fc == 31)), inc=True)
                    cT_st[fc] = tv
            return cT_st[31]

        def ln_stats_tail(stats_done):
            E.wait("scalar", sT_h, stats_done)
            last = None
            for bk, dst in [(4, 0), (5, 512), (6, 1024), (7, 1536)]:
                last = S(lambda e, bk=bk, dst=dst: e.mul(
                    statrow[0:1, dst:dst + 512], bank(bk, 0, 512, 0, 1),
                    1.0 / 4096.0), inc=True)
            E.wait("vector", sS_h, last)
            rs = None
            for hh in range(2):
                VC(lambda e, hh=hh: e.tensor_tensor(
                    statrow[0:1, 2048:2560],
                    statrow[0:1, hh * 512:hh * 512 + 512],
                    statrow[0:1, hh * 512:hh * 512 + 512], ALU.mult))
                vv = VC(lambda e, hh=hh: e.tensor_tensor(
                    statrow[0:1, 2560:3072],
                    statrow[0:1, 1024 + hh * 512:1536 + hh * 512],
                    statrow[0:1, 2048:2560], ALU.subtract))
                E.wait("scalar", sV_h, vv)
                cl = SC(lambda e, hh=hh: e.activation(
                    statrow[0:1, 2048:2560],
                    statrow[0:1, 2560:3072],
                    AF.Ln, bias=epsb[0:1, 0:1]))
                rs = SC(lambda e, hh=hh: e.activation(
                    statrow[0:1, 1024 + hh * 512:1536 + hh * 512],
                    statrow[0:1, 2048:2560],
                    AF.Exp, scale=-0.5))
                E.wait("vector", sS_h, rs)
            E.wait("tensor", sS_h, rs)
            bc = None
            for hh in range(2):
                T(lambda e, hh=hh: e.matmul(
                    bank(2 + hh), ones1f_sb[:],
                    statrow[0:1, hh * 512:hh * 512 + 512],
                    start=True, stop=True))
                bc = T(lambda e, hh=hh: e.matmul(
                    bank(4 + hh), ones1f_sb[:],
                    statrow[0:1, 1024 + hh * 512:1536 + hh * 512],
                    start=True, stop=True), inc=True)
            E.wait("scalar", sT_h, bc)
            SC(lambda e: e.copy(asm[0][:, 0:512], pt[:, 2 * 512:3 * 512]))
            SC(lambda e: e.copy(asm[0][:, 512:1024], pt[:, 3 * 512:4 * 512]))
            SC(lambda e: e.copy(asm[1][:, 0:512], pt[:, 4 * 512:5 * 512]))
            bc2 = SC(lambda e: e.copy(asm[1][:, 512:1024],
                                      pt[:, 5 * 512:6 * 512]))
            return bc2

        def ln_apply(bc_done, pm_needed, xw):
            E.wait("vector", sS_h, bc_done)
            cV_ap = {}
            for fc in range(32):
                par = fc % 2
                if fc >= 2:
                    E.wait("sync", sV_h, cV_ap[fc - 2])
                E.wait("sync", sY_h, xw.get(fc) if xw else None)
                cy = Y(xbuf[par][:], d_x[fc])
                E.wait("vector", sY_h, cy)
                VC(lambda e, par=par: e.tensor_tensor(
                    t1[:], xbuf[par][:], asm[0][:], ALU.subtract))
                cv = VC(lambda e, fc=fc: e.tensor_tensor(
                    ln[:, fc * 1024:(fc + 1) * 1024], t1[:],
                    asm[1][:], ALU.mult))
                if pm_needed:
                    cv = VC(lambda e, fc=fc: e.tensor_reduce(
                        pmsb[:, fc:fc + 1], ln[:, fc * 1024:(fc + 1) * 1024],
                        AX.X, ALU.add))
                cV_ap[fc] = cv
            return cV_ap

        def pm_allreduce(ap_last, d_ari, d_aro, sar):
            E.wait("vector", sV_h, ap_last)
            last_y = None
            for bb in range(4):
                cv = V(lambda e, bb=bb: e.tensor_scalar(
                    pmz[:, bb * 32:(bb + 1) * 32], pmsb[:],
                    bselb[:, bb:bb + 1], None, ALU.mult), inc=True)
                E.wait("sync", sV_h, cv)
                last_y = Y(d_ari.rearrange("(b p) c -> b p c", p=128)[bb],
                           pmz[:, bb * 32:(bb + 1) * 32])
            E.wait("gpsimd", sY_h, last_y)
            E.op("gpsimd", lambda e: e.collective_compute(
                "AllReduce", ALU.add, replica_groups=[list(range(NCORE))],
                ins=[d_ari[:]], outs=[d_aro[:]]),
                sem=sar, name="sar%d" % id(sar))
            E.wait("gpsimd", sar, 1)

        def qchain(d_aro, sar, wq_cnt):
            E.wait("sync", sar, 1)
            cy = Y(armall[:, 0:128].rearrange("p (b c) -> p b c", b=4),
                   d_aro.rearrange("(b p) c -> p b c", p=128))
            E.wait("vector", sY_h, cy)
            VC(lambda e: e.tensor_scalar(
                macc[:], armall[:, 0:32], bselb[:, 0:1], None, ALU.mult))
            for bb in range(1, 4):
                VC(lambda e, bb=bb: e.scalar_tensor_tensor(
                    macc[:], armall[:, bb * 32:(bb + 1) * 32],
                    bselb[:, bb:bb + 1], macc[:], ALU.mult, ALU.add))
            cv = VC(lambda e: e.tensor_scalar(
                meanb[:], macc[:], 1.0 / 2048.0, None, ALU.mult))
            E.wait("tensor", sV_h, cv)
            E.wait("tensor", sP_h, wq_cnt)
            tq = None
            for kc in range(32):
                tq = T(lambda e, kc=kc: e.matmul(
                    bank(0, 0, 1), wpan[0][:, kc * 128:(kc + 1) * 128],
                    meanb[:, kc:kc + 1], start=(kc == 0), stop=(kc == 31)),
                    inc=True)
            E.wait("scalar", sT_h, tq)
            cs = S(lambda e: e.mul(qb[:], bank(0, 0, 1),
                                   1.0 / math.sqrt(128.0)), inc=True)
            E.wait("tensor", sS_h, cs)
            tsc = T(lambda e: e.matmul(bank(1, 0, 8, 0, 1), qb[:], keysb[:],
                                       start=True, stop=True), inc=True)
            E.wait("scalar", sT_h, tsc)
            cs = S(lambda e: e.copy(srow[0:1, 0:8], bank(1, 0, 8, 0, 1)),
                   inc=True)
            E.wait("vector", sS_h, cs)
            VC(lambda e: e.tensor_reduce(srow[0:1, 8:9], srow[0:1, 0:8],
                                         AX.X, ALU.max))
            cv = VC(lambda e: e.tensor_scalar(
                srow[0:1, 16:24], srow[0:1, 0:8], srow[0:1, 8:9], None,
                ALU.subtract))
            E.wait("scalar", sV_h, cv)
            cs = SC(lambda e: e.activation(srow[0:1, 32:40], srow[0:1, 16:24],
                                           AF.Exp))
            E.wait("vector", sS_h, cs)
            VC(lambda e: e.tensor_reduce(srow[0:1, 40:41], srow[0:1, 32:40],
                                         AX.X, ALU.add))
            VC(lambda e: e.reciprocal(srow[0:1, 41:42], srow[0:1, 40:41]))
            VC(lambda e: e.tensor_scalar(
                srow[0:1, 48:56], srow[0:1, 32:40], srow[0:1, 41:42], None,
                ALU.mult))
            cv = VC(lambda e: e.tensor_copy(crowb[0:1, 0:8], srow[0:1, 48:56]))
            E.wait("tensor", sV_h, cv)
            tcc = T(lambda e: e.matmul(bank(0, 64, 1, 0, 8), crowb[0:1, 0:8],
                                       onebb[:], start=True, stop=True),
                    inc=True)
            E.wait("vector", sT_h, tcc)
            cv = V(lambda e: e.tensor_copy(ccolb[:], bank(0, 64, 1, 0, 8)),
                   inc=True)
            E.wait("tensor", sV_h, cv)
            tse = None
            for mc in range(4):
                tse = T(lambda e, mc=mc: e.matmul(
                    bank(1, 128 + mc, 1), selb[:, mc * 128:(mc + 1) * 128],
                    ccolb[:], start=True, stop=True), inc=True)
            E.wait("scalar", sT_h, tse)
            cs = S(lambda e: e.copy(coeffv[:, 0:4],
                                    pt[0:128, 640:644]), inc=True)
            return cs, tq

        def ffn(idx, xw):
            sar = sAR1 if idx == 1 else sAR2
            d_aro = d_ar1o if idx == 1 else d_ar2o
            E.wait("gpsimd", sT_h, E.cnt.get("sT", 0))
            wq_cnt = P(wpan[0][:], d_wqt.rearrange("p kc k -> p (kc k)"))
            cs_coeff, tq = qchain(d_aro, sar, wq_cnt)
            cP_dn = {}
            tdn = {}
            for mc in range(4):
                if mc == 0:
                    E.wait("gpsimd", sT_h, tq)
                if mc >= 2:
                    E.wait("gpsimd", sT_h, tdn[mc - 2])
                cP_dn[mc] = P(wpan[mc % 2][:],
                              d_dnt[mc].rearrange("p kc m -> p (kc m)"))
                E.wait("tensor", sP_h, cP_dn[mc])
                if mc == 0:
                    E.wait("tensor", sS_h, cs_coeff)
                for kc in range(32):
                    for hh in range(2):
                        tv = T(lambda e, mc=mc, kc=kc, hh=hh: e.matmul(
                            bank(mc * 2 + hh),
                            wpan[mc % 2][:, kc * 128:(kc + 1) * 128],
                            ln[:, kc * 1024 + hh * 512:
                               kc * 1024 + hh * 512 + 512],
                            start=(kc == 0), stop=(kc == 31)),
                            inc=(kc == 31 and hh == 1))
                        if tv is not None:
                            tdn[mc] = tv
            cv_hb = {}
            for mc in range(4):
                E.wait("vector", sT_h, tdn[mc])
                cv_hb[mc] = V(lambda e, mc=mc: e.tensor_scalar(
                    hb[:, mc * 1024:(mc + 1) * 1024],
                    pt[:, (mc * 2) * 512:(mc * 2 + 2) * 512],
                    coeffv[:, mc:mc + 1], None, ALU.mult), inc=True)
            cP_wt = {}
            cP_up = {}
            cS_ev = {}
            cV_up = {}
            cY_xw = {}
            tgrp = {}
            for oc in range(32):
                bp = (oc % 2) * 2
                if oc < 2:
                    E.wait("gpsimd", sT_h, tdn[2 + oc])
                else:
                    E.wait("gpsimd", sT_h, tgrp[oc - 2])
                cP_wt[oc] = P(wpan[oc % 2][:],
                              d_wtt[oc].rearrange("p kc o -> p (kc o)"))
                cP_up[oc] = P(uppan[oc % 2][:],
                              d_upt[oc].rearrange("p kc o -> p (kc o)"))
                E.wait("tensor", sP_h, cP_up[oc])
                if oc == 0:
                    E.wait("tensor", sV_h, cv_hb[3])
                if oc >= 2:
                    E.wait("tensor", sS_h, cS_ev[oc - 2])
                for kc in range(32):
                    for hh in range(2):
                        T(lambda e, oc=oc, kc=kc, hh=hh, bp=bp: e.matmul(
                            bank(bp + hh),
                            wpan[oc % 2][:, kc * 128:(kc + 1) * 128],
                            ln[:, kc * 1024 + hh * 512:
                               kc * 1024 + hh * 512 + 512],
                            start=(kc == 0), stop=False))
                for kc4 in range(4):
                    for hh in range(2):
                        tv = T(lambda e, oc=oc, kc4=kc4, hh=hh, bp=bp:
                               e.matmul(
                                   bank(bp + hh),
                                   uppan[oc % 2][:, kc4 * 128:(kc4 + 1) * 128],
                                   hb[:, kc4 * 1024 + hh * 512:
                                      kc4 * 1024 + hh * 512 + 512],
                                   start=False, stop=(kc4 == 3)),
                               inc=(kc4 == 3 and hh == 1))
                        if tv is not None:
                            tgrp[oc] = tv
                E.wait("scalar", sT_h, tgrp[oc])
                if oc >= 2:
                    E.wait("scalar", sV_h, cV_up[oc - 2])
                cS_ev[oc] = S(lambda e, oc=oc, bp=bp: e.mul(
                    tmp[oc % 2][:], pt[:, bp * 512:(bp + 2) * 512], 0.5),
                    inc=True)
                if oc >= 2:
                    E.wait("sync", sY_h, cY_xw[oc - 2])
                E.wait("sync", sY_h, xw.get(oc) if xw else None)
                cy_r = Y(xbuf[oc % 2][:], d_x[oc])
                E.wait("vector", sY_h, cy_r)
                E.wait("vector", sS_h, cS_ev[oc])
                cV_up[oc] = V(lambda e, oc=oc: e.tensor_tensor(
                    xbuf[oc % 2][:], xbuf[oc % 2][:], tmp[oc % 2][:],
                    ALU.add), inc=True)
                E.wait("sync", sV_h, cV_up[oc])
                cY_xw[oc] = Y(d_x[oc], xbuf[oc % 2][:])
            return cY_xw

        if stage >= 1:
            st1 = ln_stats_from_ln()
            bc1 = ln_stats_tail(st1)

        if stage >= 2 or stage == 99:
            E.wait("tensor", sP_h, cP_w2)
            tpre = {}
            cV_ag = {}
            cY_ag = {}
            cY_msk = {}
            cpe_pre = {}
            for j in range(n_sc):
                if j >= 2:
                    E.wait("gpsimd", sT_h, tpre[j - 2])
                cP_w1 = P(w1pan[j % 2][:],
                          d_w1sl[j].rearrange("p dc s -> p (dc s)"))
                if j >= 2:
                    E.wait("sync", sP_h, cV_ag[j - 2])
                cY_msk[j] = Y(mskb[j % 2], d_masksl[j])
                E.wait("tensor", sP_h, cP_w1)
                if j >= 2:
                    E.wait("tensor", sS_h, cpe_pre[j - 2])
                for dc in range(32):
                    tv = T(lambda e, j=j, dc=dc: e.matmul(
                        bank(j % 2), w1pan[j % 2][:, dc * 128:(dc + 1) * 128],
                        w2sb[:, dc * 512:(dc + 1) * 512],
                        start=(dc == 0), stop=(dc == 31)), inc=(dc == 31))
                    if tv is not None:
                        tpre[j] = tv
                E.wait("scalar", sT_h, tpre[j])
                if j >= 2:
                    E.wait("scalar", sP_h, cV_ag[j - 2])
                cpe_pre[j] = SC(lambda e, j=j: e.copy(petmp[j % 2][:],
                                                      bank(j % 2)))
                E.wait("gpsimd", sS_h, cpe_pre[j])
                E.wait("gpsimd", sY_h, cY_msk[j])
                if j >= 2:
                    E.wait("gpsimd", sY_h, cY_ag[j - 2])
                cV_ag[j] = E.op("gpsimd", lambda e, j=j: e.tensor_tensor(
                    agb[j % 2][:], petmp[j % 2][:], mskb[j % 2], ALU.mult),
                    sem=sP_h, name="sP")
                E.wait("sync", sP_h, cV_ag[j])
                cY_ag[j] = Y(d_agin[j * 128:(j + 1) * 128, :], agb[j % 2][:])
            E.wait("gpsimd", sY_h, cY_ag[n_sc - 1])

        if stage >= 1:
            ap1 = ln_apply(bc1, True, xw0)

        if stage >= 2 or stage == 99:
            BAR()
            E.op("gpsimd", lambda e: e.collective_compute(
                "AllGather", ALU.bypass, replica_groups=[list(range(NCORE))],
                ins=[d_agin[:]], outs=[d_agout[:]]), sem=sAG, name="sAG")
            E.wait("gpsimd", sAG, 1)
            BAR()

        if stage >= 1:
            pm_allreduce(ap1[31], d_ar1i, d_ar1o, sAR1)
        BAR()

        xwf = xw0
        if stage >= 1 or stage == 99:
            xw1 = ffn(1, xw0)
            xwf = xw1
        if stage >= 2 and stage != 99:
            st2 = ln_stats_from_x(xw1)
            bc2 = ln_stats_tail(st2)
            ap2 = ln_apply(bc2, False, xw1)
        BAR()

        if stage < 2 or stage == 99:
            xw_attn = None
        if stage >= 2 and stage != 99:
         E.wait("tensor", sAG, 1)
         E.wait("sync", sAG, 1)
         ngrp = (n_sc + 7) // 8
         agr = d_agout.rearrange("(r j p) t -> r p j t", p=128, j=n_sc)
         cY_agl = {}
         tgrp = {}
         cS_gel = {}
         cY_axw = {}
         cVu_att = {}
         for r in range(NCORE):
             for g in range(ngrp):
                 seq = r * ngrp + g
                 ns_g = min(8, n_sc - g * 8)
                 if seq >= 2:
                     E.wait("sync", sT_h, tgrp[seq - 2])
                 cY_agl[seq] = Y(
                     wpan[seq % 2][:, 0:ns_g * 512].rearrange(
                         "p (j t) -> p j t", j=ns_g),
                     agr[r][:, g * 8:g * 8 + ns_g, :])
                 E.wait("tensor", sY_h, cY_agl[seq])
                 if g == 0 and r >= 1:
                     E.wait("tensor", sS_h, cS_gel[(r - 1, 7)])
                 for jj in range(ns_g):
                     j = g * 8 + jj
                     gkc = SCL[r][j]
                     for tcl in range(4):
                         for hh in range(2):
                             bk = tcl * 2 + hh
                             tv = T(lambda e, seq=seq, jj=jj, tcl=tcl, hh=hh,
                                    bk=bk, gkc=gkc, f=(j == 0),
                                    l=(j == n_sc - 1): e.matmul(
                                        bank(bk),
                                        wpan[seq % 2][:, jj * 512 + tcl * 128:
                                                      jj * 512 + tcl * 128 + 128],
                                        ln[:, gkc * 1024 + hh * 512:
                                           gkc * 1024 + hh * 512 + 512],
                                        start=f, stop=l),
                                    inc=(jj == ns_g - 1 and tcl == 3
                                         and hh == 1))
                             if tv is not None:
                                 tgrp[seq] = tv
             for tcl in range(4):
                 tc = r * 4 + tcl
                 par = tc % 2
                 gb = gbuf[tc % 4]
                 E.wait("scalar", sT_h, tgrp[r * ngrp + ngrp - 1])
                 E.wait("scalar", sV_h, cVu_att.get(tc - 4))
                 for hh in range(2):
                     cS_gel[(r, tcl * 2 + hh)] = S(
                         lambda e, tcl=tcl, hh=hh, tc=tc, gb=gb:
                         e.activation(
                             gb[:, hh * 512:hh * 512 + 512],
                             bank(tcl * 2 + hh), AF.Gelu,
                             bias=b2sb[:, tc:tc + 1]), inc=True)
                 seq2 = r * 4 + tcl
                 if seq2 >= 2:
                     E.wait("sync", sY_h, cY_axw[seq2 - 2])
                 E.wait("sync", sY_h, xw1[tc])
                 cy_r = Y(xbuf[par][:], d_x[tc])
                 E.wait("vector", sY_h, cy_r)
                 E.wait("vector", sS_h, cS_gel[(r, tcl * 2 + 1)])
                 cvu = V(lambda e, par=par, gb=gb: e.tensor_tensor(
                     xbuf[par][:], xbuf[par][:], gb[:], ALU.add),
                     inc=True)
                 cVu_att[tc] = cvu
                 E.wait("sync", sV_h, cvu)
                 cY_axw[seq2] = Y(d_x[tc], xbuf[par][:])
         xw_attn = {r * 4 + tcl: cY_axw[r * 4 + tcl]
                    for r in range(NCORE) for tcl in range(4)}
         xwf = xw_attn
         if stage >= 3:
            st3 = ln_stats_from_x(xw_attn)
            bc3 = ln_stats_tail(st3)
            ap3 = ln_apply(bc3, True, xw_attn)
            pm_allreduce(ap3[31], d_ar2i, d_ar2o, sAR2)
         BAR()

        if stage >= 3 and stage != 99:
            xw2 = ffn(2, xw_attn)
            xwf = xw2
        BAR()

        outr = d_out.rearrange("(tb p) f -> p tb f", p=128)
        cY_ow = {}
        cT_ot = {}
        cS_ob = {}
        for fc in range(32):
            par = fc % 2
            if fc >= 2:
                E.wait("gpsimd", sT_h, cT_ot[fc - 2])
            E.wait("gpsimd", sY_h, xwf[fc])
            cy = P(xbuf[par][:], d_x[fc])
            E.wait("tensor", sP_h, cy)
            E.wait("tensor", sS_h, cS_ob.get(fc - 1))
            for tb in range(8):
                tv = T(lambda e, par=par, tb=tb: e.transpose(
                    bank(tb, 0, 128), xbuf[par][:, tb * 128:(tb + 1) * 128],
                    idn_sb[:]), inc=(tb == 7))
                if tb == 7:
                    cT_ot[fc] = tv
            E.wait("scalar", sT_h, cT_ot[fc])
            E.wait("scalar", sY_h, cY_ow.get(fc - 2))
            sv = None
            for tb in range(8):
                sv = S(lambda e, par=par, tb=tb: e.copy(
                    tmp[par][:, tb * 128:(tb + 1) * 128], bank(tb, 0, 128)),
                    inc=(tb == 7))
            cS_ob[fc] = sv
            E.wait("sync", sS_h, cS_ob[fc])
            cY_ow[fc] = Y(outr[:, :, fc * 128:(fc + 1) * 128],
                          tmp[par].rearrange("p (tb t) -> p tb t", tb=8))
        if dbg:
            for fc in range(32):
                Y(d_dbg3[fc], ln[:, fc * 1024:(fc + 1) * 1024])
            Y(d_dbg2[:, 0:1024], asm[0][:])
            Y(d_dbg2[:, 1024:2048], asm[1][:])
            dv = V(lambda e: e.tensor_copy(armall[:, 120:124], coeffv[:, 0:4]),
                   inc=True)
            dv = V(lambda e: e.tensor_copy(armall[0:8, 124:125], ccolb[:]),
                   inc=True)
            dv = V(lambda e: e.tensor_copy(armall[0:1, 64:120],
                                           srow[0:1, 0:56]), inc=True)
            E.wait("sync", sV_h, dv)
            Y(d_dbg[:, 0:32], macc[:])
            Y(d_dbg[:, 32:64], pmsb[:])
            Y(d_dbg[:, 64:128], armall[:, 64:128])
        E.wait("sync", sY_h, E.cnt["sY"])
        E.wait("gpsimd", sP_h, E.cnt["sP"])
        BAR()

        build._last_E = E
        with nc.Block() as block:
            for eng_name in ENGS:
                def make(eng_name):
                    def body(e):
                        for fn in E.q[eng_name]:
                            fn(e)
                    return body
                getattr(block, eng_name)(make(eng_name))

    return nc


def run_staged(inputs, stage=3, dbg=False, want_dbg=False):
    in_maps, SCL, n_sc = _host_prep(inputs)
    nc = build(SCL, n_sc, stage=stage, dbg=dbg)
    res = run_bass_kernel_spmd(nc, in_maps, core_ids=list(range(NCORE)))
    out = np.empty((4, 2048, 4096), dtype=np.float32)
    for c in range(NCORE):
        b, h = c // 2, c % 2
        out[b, h * 1024:(h + 1) * 1024, :] = res.results[c]["out"]
    if want_dbg:
        return out, [res.results[c]["dbg"] for c in range(NCORE)], \
            [res.results[c]["dbg2"] for c in range(NCORE)], \
            [res.results[c]["dbg3"] for c in range(NCORE)]
    return out


def kernel(**inputs):
    return run_staged(inputs, stage=3)


# version 06
# speedup vs baseline: 1.0714x; 1.0714x over previous
import math
import numpy as np
import concourse.bass as bass
import concourse.mybir as mybir
from concourse.bass_utils import run_bass_kernel_spmd

F32 = mybir.dt.float32
BF16 = mybir.dt.bfloat16
AF = mybir.ActivationFunctionType
ALU = mybir.AluOpType
AX = mybir.AxisListType

NCORE = 8
ENGS = ("tensor", "vector", "scalar", "gpsimd", "sync")


class Emit:

    def __init__(self):
        self.q = {e: [] for e in ENGS}
        self.cnt = {}

    def on(self, eng, fn):
        self.q[eng].append(fn)

    def wait(self, eng, sem, val):
        if val is None or val <= 0:
            return
        self.on(eng, lambda e, s=sem, v=val: e.wait_ge(s, v))

    def dma(self, eng, sem, name, out, in_):
        self.cnt[name] = self.cnt.get(name, 0) + 16
        v = self.cnt[name]
        self.on(eng, lambda e, o=out, i=in_, s=sem:
                e.dma_start(out=o, in_=i).then_inc(s, 16))
        return v

    def op(self, eng, fn, sem=None, name=None):
        if sem is not None:
            self.cnt[name] = self.cnt.get(name, 0) + 1
            v = self.cnt[name]
            self.on(eng, lambda e, f=fn, s=sem: f(e).then_inc(s, 1))
            return v
        self.on(eng, lambda e, f=fn: f(e))
        return None

    def barrier(self, bar, selfw=()):
        for eng, semh, nm in selfw:
            self.wait(eng, semh, self.cnt.get(nm, 0))
        self.cnt["bar"] = self.cnt.get("bar", 0) + 5
        v = self.cnt["bar"]
        for e in ENGS:
            self.on(e, lambda en, s=bar: en.nop().then_inc(s, 1))
        for e in ENGS:
            self.wait(e, bar, v)
        return v


def _host_prep(inputs):
    x = np.ascontiguousarray(np.asarray(inputs["L0_attn_x"], dtype=np.float32))
    w1 = np.asarray(inputs["L0_attn_w1"], dtype=np.float32)
    w2 = np.asarray(inputs["L0_attn_w2"], dtype=np.float32)
    b2 = np.asarray(inputs["L0_attn_b2"], dtype=np.float32)
    mask = np.asarray(inputs["L0_attn_sparse_mask"], dtype=np.float32)
    W_org = np.asarray(inputs["L0_ffn_W_org"], dtype=np.float32)
    dn = np.asarray(inputs["L0_ffn_lora_down"], dtype=np.float32)
    up = np.asarray(inputs["L0_ffn_lora_up"], dtype=np.float32)
    keys = np.asarray(inputs["L0_ffn_lora_keys"], dtype=np.float32)
    Wq = np.asarray(inputs["L0_ffn_Wq"], dtype=np.float32)

    Bm = mask.reshape(32, 128, 32, 128).any(axis=(1, 3))
    scl = []
    for r in range(NCORE):
        lst = [sc for sc in range(32) if Bm[sc, 4 * r:4 * r + 4].any()]
        if not lst:
            lst = [0]
        scl.append(lst)
    n_sc = max(len(l) for l in scl)

    shared = {}
    shared["wtt"] = np.ascontiguousarray(
        W_org.T.reshape(32, 128, 32, 128).transpose(2, 1, 0, 3))
    D = dn.transpose(1, 0, 2).reshape(4096, 512)
    shared["dnt"] = np.ascontiguousarray(
        D.reshape(32, 128, 4, 128).transpose(2, 1, 0, 3))
    U = up.reshape(512, 4096)
    shared["upt"] = np.ascontiguousarray(
        U.reshape(4, 128, 32, 128).transpose(2, 1, 0, 3))
    shared["wqt"] = np.ascontiguousarray(
        Wq.T.reshape(32, 128, 128).transpose(1, 0, 2))
    shared["keyst"] = np.ascontiguousarray(keys.T)
    shared["b2c"] = np.ascontiguousarray(b2.reshape(32, 128).T)
    shared["idn"] = np.eye(128, dtype=np.float32)
    shared["onesc"] = np.ones((128, 1), dtype=np.float32)
    shared["ones1f"] = np.ones((1, 128), dtype=np.float32)
    shared["onebf"] = np.ones((1, 1), dtype=np.float32)
    shared["epsc"] = np.full((1, 1), 1e-5, dtype=np.float32)
    sel8 = np.zeros((8, 512), dtype=np.float32)
    for j in range(512):
        sel8[j // 64, j] = 1.0
    shared["sel8"] = sel8

    W1R = w1.reshape(32, 128, 32, 128).transpose(2, 1, 0, 3)
    W2T = np.ascontiguousarray(w2.T.reshape(32, 128, 4096))

    in_maps = []
    for c in range(NCORE):
        b, h = c // 2, c % 2
        t0 = 512 * c
        lst = scl[c]
        w1sl = np.empty((n_sc, 128, 32, 128), dtype=np.float32)
        msl = np.zeros((n_sc, 128, 512), dtype=np.float32)
        for j in range(n_sc):
            sc = lst[j] if j < len(lst) else lst[0]
            w1sl[j] = W1R[sc]
            if j < len(lst):
                msl[j] = mask[sc * 128:(sc + 1) * 128, t0:t0 + 512]
        bsel = np.zeros((128, 4), dtype=np.float32)
        bsel[:, b] = 1.0
        m = dict(shared)
        m["xtm"] = np.ascontiguousarray(x[b, h * 1024:(h + 1) * 1024, :])
        m["w1sl"] = w1sl
        m["masksl"] = msl
        m["w2ts"] = np.ascontiguousarray(W2T[:, :, t0:t0 + 512])
        m["bselc"] = bsel
        in_maps.append(m)

    SCL = [[scl[r][j] if j < len(scl[r]) else scl[r][0] for j in range(n_sc)]
           for r in range(NCORE)]
    return in_maps, SCL, n_sc


def build(SCL, n_sc, stage=3, dbg=False):
    nc = bass.Bass("TRN2", target_bir_lowering=False)

    d_xtm = nc.dram_tensor("xtm", [1024, 4096], F32, kind="ExternalInput")
    d_wtt = nc.dram_tensor("wtt", [32, 128, 32, 128], F32, kind="ExternalInput")
    d_dnt = nc.dram_tensor("dnt", [4, 128, 32, 128], F32, kind="ExternalInput")
    d_upt = nc.dram_tensor("upt", [32, 128, 4, 128], F32, kind="ExternalInput")
    d_wqt = nc.dram_tensor("wqt", [128, 32, 128], F32, kind="ExternalInput")
    d_keyst = nc.dram_tensor("keyst", [128, 8], F32, kind="ExternalInput")
    d_b2c = nc.dram_tensor("b2c", [128, 32], F32, kind="ExternalInput")
    d_idn = nc.dram_tensor("idn", [128, 128], F32, kind="ExternalInput")
    d_onesc = nc.dram_tensor("onesc", [128, 1], F32, kind="ExternalInput")
    d_ones1f = nc.dram_tensor("ones1f", [1, 128], F32, kind="ExternalInput")
    d_onebf = nc.dram_tensor("onebf", [1, 1], F32, kind="ExternalInput")
    d_epsc = nc.dram_tensor("epsc", [1, 1], F32, kind="ExternalInput")
    d_sel8 = nc.dram_tensor("sel8", [8, 512], F32, kind="ExternalInput")
    d_w1sl = nc.dram_tensor("w1sl", [n_sc, 128, 32, 128], F32,
                            kind="ExternalInput")
    d_masksl = nc.dram_tensor("masksl", [n_sc, 128, 512], F32,
                              kind="ExternalInput")
    d_w2ts = nc.dram_tensor("w2ts", [32, 128, 512], F32, kind="ExternalInput")
    d_bselc = nc.dram_tensor("bselc", [128, 4], F32, kind="ExternalInput")
    d_out = nc.dram_tensor("out", [1024, 4096], F32, kind="ExternalOutput")
    d_dbg = nc.dram_tensor("dbg", [128, 128], F32, kind="ExternalOutput")
    d_dbg2 = nc.dram_tensor("dbg2", [128, 4096], F32, kind="ExternalOutput")
    d_dbg3 = nc.dram_tensor("dbg3", [32, 128, 1024], BF16, kind="ExternalOutput")

    d_x = nc.dram_tensor("x_home", [32, 128, 1024], F32)
    d_agin = nc.dram_tensor("ag_in", [n_sc * 128, 512], BF16)
    d_agout = nc.dram_tensor("ag_out", [NCORE * n_sc * 128, 512], BF16,
                             addr_space="Shared")
    d_ar1i = nc.dram_tensor("ar1_in", [512, 32], F32)
    d_ar1o = nc.dram_tensor("ar1_out", [512, 32], F32, addr_space="Shared")
    d_ar2i = nc.dram_tensor("ar2_in", [512, 32], F32)
    d_ar2o = nc.dram_tensor("ar2_out", [512, 32], F32, addr_space="Shared")

    import contextlib
    with contextlib.ExitStack() as st:
        def sb(name, shape, dt):
            return st.enter_context(nc.sbuf_tensor(name, shape, dt))

        def sem(name):
            return st.enter_context(nc.semaphore(name))

        ln = sb("ln", [128, 32768], BF16)
        w2sb = sb("w2sb", [128, 16384], BF16)
        wpan = [sb(f"wpan{i}", [128, 4096], BF16) for i in range(2)]
        w1pan = [sb(f"w1pan{i}", [128, 4096], BF16) for i in range(2)]
        uppan = [sb(f"uppan{i}", [128, 512], BF16) for i in range(2)]
        agb = uppan
        xinb = [sb(f"xinb{i}", [128, 1024], F32) for i in range(2)]
        mskb = [xinb[0][:, 0:512], xinb[1][:, 0:512]]
        asm = [sb(f"asm{i}", [128, 1024], F32) for i in range(2)]
        xbuf = [sb(f"xbuf{i}", [128, 1024], F32) for i in range(2)]
        tmp = [sb(f"tmp{i}", [128, 1024], F32) for i in range(2)]
        xbb = [sb(f"xbb{i}", [128, 1024], BF16) for i in range(2)]
        sqb = [sb(f"sqb{i}", [128, 1024], BF16) for i in range(2)]
        hb = sb("hb", [128, 4096], BF16)
        petmp = [sb(f"petmp{i}", [128, 512], F32) for i in range(2)]
        gtmp = [sb(f"gtmp{i}", [128, 1024], F32) for i in range(2)]
        gbuf = [tmp[0], tmp[1], gtmp[0], gtmp[1]]
        statrow = sb("statrow", [1, 3072], F32)
        t1 = tmp[0]
        pmsb = sb("pmsb", [128, 32], F32)
        pmz = sb("pmz", [128, 128], F32)
        armall = sb("armall", [128, 128], F32)
        macc = sb("macc", [128, 32], F32)
        meanb = sb("meanb", [128, 32], BF16)
        qb = sb("qb", [128, 1], BF16)
        srow = sb("srow", [1, 64], F32)
        crowb = sb("crowb", [1, 8], BF16)
        ccolb = sb("ccolb", [8, 1], BF16)
        coeffv = sb("coeffv", [128, 4], F32)
        b2sb = sb("b2sb", [128, 32], F32)
        bselb = sb("bselb", [128, 4], F32)
        idn_sb = sb("idn_sb", [128, 128], F32)
        onesb = sb("onesb", [128, 1], BF16)
        ones1f_sb = sb("ones1f_sb", [1, 128], F32)
        onebb = sb("onebb", [1, 1], BF16)
        selb = sb("selb", [8, 512], BF16)
        keysb = sb("keysb", [128, 8], BF16)
        epsb = sb("epsb", [1, 1], F32)

        pt = st.enter_context(nc.psum_tensor("pt", [128, 4096], F32))

        def bank(i, lo=0, n=512, p0=0, pn=128):
            return pt[p0:p0 + pn, i * 512 + lo:i * 512 + lo + n]

        sY_h = sem("sY")
        sP_h = sem("sP")
        sT_h = sem("sT")
        sS_h = sem("sS")
        sV_h = sem("sV")
        bar_h = sem("bar")
        sSD_h = sem("sSD")
        sAG = sem("sAG")
        sAR1 = sem("sAR1")
        sAR2 = sem("sAR2")

        E = Emit()

        def BAR():
            return E.barrier(bar_h, selfw=(
                ("vector", sV_h, "sV"), ("scalar", sS_h, "sS"),
                ("tensor", sT_h, "sT")))

        def Y(out, in_):
            return E.dma("sync", sY_h, "sY", out, in_)

        def P(out, in_):
            return E.dma("gpsimd", sP_h, "sP", out, in_)

        def SD(out, in_):
            return E.dma("scalar", sSD_h, "sSD", out, in_)

        def T(fn, inc=False):
            return E.op("tensor", fn, sem=sT_h if inc else None, name="sT")

        def S(fn, inc=False):
            return E.op("scalar", fn, sem=sS_h if inc else None, name="sS")

        def V(fn, inc=False):
            return E.op("vector", fn, sem=sV_h if inc else None, name="sV")

        def VC(fn):
            E.wait("vector", sV_h, E.cnt.get("sV", 0))
            return V(fn, inc=True)

        def SC(fn):
            E.wait("scalar", sS_h, E.cnt.get("sS", 0))
            return S(fn, inc=True)

        for di, so in [(d_onesc[:], onesb[:]), (d_onebf[:], onebb[:]),
                       (d_sel8[:], selb[:]), (d_keyst[:], keysb[:])]:
            P(so, di)
        for di, so in [(d_idn[:], idn_sb[:]), (d_ones1f[:], ones1f_sb[:]),
                       (d_b2c[:], b2sb[:]), (d_bselc[:], bselb[:]),
                       (d_epsc[:], epsb[:])]:
            Y(so, di)
        cY_const = E.cnt["sY"]
        for dc in range(32):
            P(w2sb[:, dc * 512:(dc + 1) * 512], d_w2ts[dc])
        cP_w2 = E.cnt["sP"]

        E.wait("tensor", sY_h, cY_const)
        xdr = d_x.rearrange("fc p t -> p fc t")
        cY_xin = {}
        cS_asm = {}
        cV_lnp = {}
        cY_asmw = {}
        cT_tp = {}
        for tb in range(8):
            for fq in range(4):
                k = tb * 4 + fq
                par = k % 2
                if k >= 2:
                    E.wait("sync", sT_h, cT_tp[k - 2])
                cY_xin[k] = Y(xinb[par][:],
                              d_xtm[tb * 128:(tb + 1) * 128,
                                    fq * 1024:(fq + 1) * 1024])
                E.wait("tensor", sY_h, cY_xin[k])
                if k >= 2:
                    E.wait("tensor", sY_h, cY_asmw[k - 2])
                    E.wait("tensor", sV_h, cV_lnp[k - 2])
                E.wait("tensor", sS_h, cS_asm.get(k - 1))
                for i in range(8):
                    tv = T(lambda e, par=par, i=i: e.transpose(
                        bank(i, 0, 128), xinb[par][:, i * 128:(i + 1) * 128],
                        idn_sb[:]), inc=(i == 7))
                    if i == 7:
                        cT_tp[k] = tv
                E.wait("scalar", sT_h, cT_tp[k])
                sv = None
                for i in range(8):
                    sv = S(lambda e, par=par, i=i: e.copy(
                        asm[par][:, i * 128:(i + 1) * 128], bank(i, 0, 128)),
                        inc=(i == 7))
                cS_asm[k] = sv
                E.wait("sync", sS_h, cS_asm[k])
                cY_asmw[k] = Y(
                    xdr[:, fq * 8:(fq + 1) * 8, tb * 128:(tb + 1) * 128],
                    asm[par].rearrange("p (fc t) -> p fc t", fc=8))
                E.wait("vector", sS_h, cS_asm[k])
                v = None
                for i in range(8):
                    fc = fq * 8 + i
                    v = V(lambda e, par=par, i=i, fc=fc, tb=tb: e.tensor_copy(
                        ln[:, fc * 1024 + tb * 128:fc * 1024 + (tb + 1) * 128],
                        asm[par][:, i * 128:(i + 1) * 128]), inc=(i == 7))
                cV_lnp[k] = v
        xw0 = {fc: cY_asmw[28 + fc // 8] for fc in range(32)}
        BAR()

        def ln_stats_from_ln():
            cT_st = {}
            for fc in range(32):
                E.wait("scalar", sT_h, cT_st.get(fc - 2))
                cs = S(lambda e, fc=fc: e.activation(
                    sqb[fc % 2][:], ln[:, fc * 1024:(fc + 1) * 1024],
                    AF.Square), inc=True)
                E.wait("tensor", sS_h, cs)
                for hh in range(2):
                    T(lambda e, fc=fc, hh=hh: e.matmul(
                        bank(4 + hh, 0, 512, 0, 1), onesb[:],
                        ln[:, fc * 1024 + hh * 512:fc * 1024 + hh * 512 + 512],
                        start=(fc == 0), stop=(fc == 31)))
                    tv = T(lambda e, fc=fc, hh=hh: e.matmul(
                        bank(6 + hh, 0, 512, 0, 1), onesb[:],
                        sqb[fc % 2][:, hh * 512:hh * 512 + 512],
                        start=(fc == 0), stop=(fc == 31)), inc=True)
                    cT_st[fc] = tv
            return cT_st[31]

        def ln_stats_from_x(xw):
            cT_st = {}
            cV_done = {}
            for fc in range(32):
                par = fc % 2
                E.wait("sync", sY_h, xw.get(fc) if xw else None)
                if fc >= 2:
                    E.wait("sync", sS_h, cV_done[fc - 2])
                cy = Y(xbuf[par][:], d_x[fc])
                E.wait("scalar", sY_h, cy)
                E.wait("scalar", sT_h, cT_st.get(fc - 2))
                SC(lambda e, par=par: e.copy(xbb[par][:], xbuf[par][:]))
                cs = SC(lambda e, par=par: e.activation(
                    sqb[par][:], xbb[par][:], AF.Square))
                cV_done[fc] = cs
                E.wait("tensor", sS_h, cs)
                for hh in range(2):
                    T(lambda e, par=par, hh=hh: e.matmul(
                        bank(4 + hh, 0, 512, 0, 1), onesb[:],
                        xbb[par][:, hh * 512:hh * 512 + 512],
                        start=(fc == 0), stop=(fc == 31)))
                    tv = T(lambda e, par=par, hh=hh: e.matmul(
                        bank(6 + hh, 0, 512, 0, 1), onesb[:],
                        sqb[par][:, hh * 512:hh * 512 + 512],
                        start=(fc == 0), stop=(fc == 31)), inc=True)
                    cT_st[fc] = tv
            return cT_st[31]

        def ln_stats_tail(stats_done):
            E.wait("scalar", sT_h, stats_done)
            last = None
            for bk, dst in [(4, 0), (5, 512), (6, 1024), (7, 1536)]:
                last = S(lambda e, bk=bk, dst=dst: e.mul(
                    statrow[0:1, dst:dst + 512], bank(bk, 0, 512, 0, 1),
                    1.0 / 4096.0), inc=True)
            E.wait("vector", sS_h, last)
            rs = None
            for hh in range(2):
                VC(lambda e, hh=hh: e.tensor_tensor(
                    statrow[0:1, 2048:2560],
                    statrow[0:1, hh * 512:hh * 512 + 512],
                    statrow[0:1, hh * 512:hh * 512 + 512], ALU.mult))
                vv = VC(lambda e, hh=hh: e.tensor_tensor(
                    statrow[0:1, 2560:3072],
                    statrow[0:1, 1024 + hh * 512:1536 + hh * 512],
                    statrow[0:1, 2048:2560], ALU.subtract))
                E.wait("scalar", sV_h, vv)
                cl = SC(lambda e, hh=hh: e.activation(
                    statrow[0:1, 2048:2560],
                    statrow[0:1, 2560:3072],
                    AF.Ln, bias=epsb[0:1, 0:1]))
                rs = SC(lambda e, hh=hh: e.activation(
                    statrow[0:1, 1024 + hh * 512:1536 + hh * 512],
                    statrow[0:1, 2048:2560],
                    AF.Exp, scale=-0.5))
                E.wait("vector", sS_h, rs)
            E.wait("tensor", sS_h, rs)
            bc = None
            for hh in range(2):
                T(lambda e, hh=hh: e.matmul(
                    bank(2 + hh), ones1f_sb[:],
                    statrow[0:1, hh * 512:hh * 512 + 512],
                    start=True, stop=True))
                bc = T(lambda e, hh=hh: e.matmul(
                    bank(4 + hh), ones1f_sb[:],
                    statrow[0:1, 1024 + hh * 512:1536 + hh * 512],
                    start=True, stop=True), inc=True)
            E.wait("scalar", sT_h, bc)
            SC(lambda e: e.copy(asm[0][:, 0:512], pt[:, 2 * 512:3 * 512]))
            SC(lambda e: e.copy(asm[0][:, 512:1024], pt[:, 3 * 512:4 * 512]))
            SC(lambda e: e.copy(asm[1][:, 0:512], pt[:, 4 * 512:5 * 512]))
            bc2 = SC(lambda e: e.copy(asm[1][:, 512:1024],
                                      pt[:, 5 * 512:6 * 512]))
            return bc2

        def ln_apply(bc_done, pm_needed, xw):
            E.wait("vector", sS_h, bc_done)
            cV_ap = {}
            for fc in range(32):
                par = fc % 2
                if fc >= 2:
                    E.wait("sync", sV_h, cV_ap[fc - 2])
                E.wait("sync", sY_h, xw.get(fc) if xw else None)
                cy = Y(xbuf[par][:], d_x[fc])
                E.wait("vector", sY_h, cy)
                VC(lambda e, par=par: e.tensor_tensor(
                    t1[:], xbuf[par][:], asm[0][:], ALU.subtract))
                cv = VC(lambda e, fc=fc: e.tensor_tensor(
                    ln[:, fc * 1024:(fc + 1) * 1024], t1[:],
                    asm[1][:], ALU.mult))
                if pm_needed:
                    cv = VC(lambda e, fc=fc: e.tensor_reduce(
                        pmsb[:, fc:fc + 1], ln[:, fc * 1024:(fc + 1) * 1024],
                        AX.X, ALU.add))
                cV_ap[fc] = cv
            return cV_ap

        def pm_allreduce(ap_last, d_ari, d_aro, sar):
            E.wait("vector", sV_h, ap_last)
            last_y = None
            for bb in range(4):
                cv = V(lambda e, bb=bb: e.tensor_scalar(
                    pmz[:, bb * 32:(bb + 1) * 32], pmsb[:],
                    bselb[:, bb:bb + 1], None, ALU.mult), inc=True)
                E.wait("sync", sV_h, cv)
                last_y = Y(d_ari.rearrange("(b p) c -> b p c", p=128)[bb],
                           pmz[:, bb * 32:(bb + 1) * 32])
            E.wait("gpsimd", sY_h, last_y)
            E.op("gpsimd", lambda e: e.collective_compute(
                "AllReduce", ALU.add, replica_groups=[list(range(NCORE))],
                ins=[d_ari[:]], outs=[d_aro[:]]),
                sem=sar, name="sar%d" % id(sar))
            E.wait("gpsimd", sar, 1)

        def qchain(d_aro, sar, wq_cnt):
            E.wait("sync", sar, 1)
            cy = Y(armall[:, 0:128].rearrange("p (b c) -> p b c", b=4),
                   d_aro.rearrange("(b p) c -> p b c", p=128))
            E.wait("vector", sY_h, cy)
            VC(lambda e: e.tensor_scalar(
                macc[:], armall[:, 0:32], bselb[:, 0:1], None, ALU.mult))
            for bb in range(1, 4):
                VC(lambda e, bb=bb: e.scalar_tensor_tensor(
                    macc[:], armall[:, bb * 32:(bb + 1) * 32],
                    bselb[:, bb:bb + 1], macc[:], ALU.mult, ALU.add))
            cv = VC(lambda e: e.tensor_scalar(
                meanb[:], macc[:], 1.0 / 2048.0, None, ALU.mult))
            E.wait("tensor", sV_h, cv)
            E.wait("tensor", sP_h, wq_cnt)
            tq = None
            for kc in range(32):
                tq = T(lambda e, kc=kc: e.matmul(
                    bank(0, 0, 1), wpan[0][:, kc * 128:(kc + 1) * 128],
                    meanb[:, kc:kc + 1], start=(kc == 0), stop=(kc == 31)),
                    inc=True)
            E.wait("scalar", sT_h, tq)
            cs = S(lambda e: e.mul(qb[:], bank(0, 0, 1),
                                   1.0 / math.sqrt(128.0)), inc=True)
            E.wait("tensor", sS_h, cs)
            tsc = T(lambda e: e.matmul(bank(1, 0, 8, 0, 1), qb[:], keysb[:],
                                       start=True, stop=True), inc=True)
            E.wait("scalar", sT_h, tsc)
            cs = S(lambda e: e.copy(srow[0:1, 0:8], bank(1, 0, 8, 0, 1)),
                   inc=True)
            E.wait("vector", sS_h, cs)
            VC(lambda e: e.tensor_reduce(srow[0:1, 8:9], srow[0:1, 0:8],
                                         AX.X, ALU.max))
            cv = VC(lambda e: e.tensor_scalar(
                srow[0:1, 16:24], srow[0:1, 0:8], srow[0:1, 8:9], None,
                ALU.subtract))
            E.wait("scalar", sV_h, cv)
            cs = SC(lambda e: e.activation(srow[0:1, 32:40], srow[0:1, 16:24],
                                           AF.Exp))
            E.wait("vector", sS_h, cs)
            VC(lambda e: e.tensor_reduce(srow[0:1, 40:41], srow[0:1, 32:40],
                                         AX.X, ALU.add))
            VC(lambda e: e.reciprocal(srow[0:1, 41:42], srow[0:1, 40:41]))
            VC(lambda e: e.tensor_scalar(
                srow[0:1, 48:56], srow[0:1, 32:40], srow[0:1, 41:42], None,
                ALU.mult))
            cv = VC(lambda e: e.tensor_copy(crowb[0:1, 0:8], srow[0:1, 48:56]))
            E.wait("tensor", sV_h, cv)
            tcc = T(lambda e: e.matmul(bank(0, 64, 1, 0, 8), crowb[0:1, 0:8],
                                       onebb[:], start=True, stop=True),
                    inc=True)
            E.wait("vector", sT_h, tcc)
            cv = V(lambda e: e.tensor_copy(ccolb[:], bank(0, 64, 1, 0, 8)),
                   inc=True)
            E.wait("tensor", sV_h, cv)
            tse = None
            for mc in range(4):
                tse = T(lambda e, mc=mc: e.matmul(
                    bank(1, 128 + mc, 1), selb[:, mc * 128:(mc + 1) * 128],
                    ccolb[:], start=True, stop=True), inc=True)
            E.wait("scalar", sT_h, tse)
            cs = S(lambda e: e.copy(coeffv[:, 0:4],
                                    pt[0:128, 640:644]), inc=True)
            return cs, tq

        def ffn(idx, xw, ap, ag_free=None):
            sar = sAR1 if idx == 1 else sAR2
            d_ari = d_ar1i if idx == 1 else d_ar2i
            d_aro = d_ar1o if idx == 1 else d_ar2o
            E.wait("gpsimd", sT_h, E.cnt.get("sT", 0))
            panels = [wpan[0], wpan[1], w1pan[0], w1pan[1]]
            cP_dn = 0
            for mc in range(4):
                cP_dn = P(panels[mc][:],
                          d_dnt[mc].rearrange("p kc m -> p (kc m)"))
            E.wait("tensor", sP_h, cP_dn)
            tdn = None
            for kc in range(32):
                E.wait("tensor", sV_h, ap[kc])
                for mc in range(4):
                    for hh in range(2):
                        tdn = T(lambda e, mc=mc, kc=kc, hh=hh: e.matmul(
                            bank(mc * 2 + hh),
                            panels[mc][:, kc * 128:(kc + 1) * 128],
                            ln[:, kc * 1024 + hh * 512:
                               kc * 1024 + hh * 512 + 512],
                            start=(kc == 0), stop=(kc == 31)),
                            inc=(kc == 31 and mc == 3 and hh == 1))
            c_hbraw = None
            E.wait("vector", sT_h, tdn)
            for mc in range(4):
                c_hbraw = VC(lambda e, mc=mc: e.tensor_copy(
                    hb[:, mc * 1024:(mc + 1) * 1024],
                    pt[:, (mc * 2) * 512:(mc * 2 + 2) * 512]))
            pm_allreduce(ap[31], d_ari, d_aro, sar)
            E.wait("gpsimd", sT_h, E.cnt.get("sT", 0))
            wq_cnt = P(wpan[0][:], d_wqt.rearrange("p kc k -> p (kc k)"))
            cs_coeff, tq = qchain(d_aro, sar, wq_cnt)
            E.wait("vector", sS_h, cs_coeff)
            c_scale = None
            for mc in range(4):
                c_scale = VC(lambda e, mc=mc: e.tensor_scalar(
                    hb[:, mc * 1024:(mc + 1) * 1024],
                    hb[:, mc * 1024:(mc + 1) * 1024],
                    coeffv[:, mc:mc + 1], None, ALU.mult))
            cP_wt = {}
            cP_up = {}
            cS_ev = {}
            cV_up = {}
            cY_xw = {}
            tgrp = {}
            for oc in range(32):
                bp = (oc % 2) * 2
                if oc < 2:
                    E.wait("gpsimd", sT_h, tq)
                    if oc == 0 and ag_free is not None:
                        E.wait("gpsimd", sSD_h, ag_free)
                else:
                    E.wait("gpsimd", sT_h, tgrp[oc - 2])
                cP_wt[oc] = P(wpan[oc % 2][:],
                              d_wtt[oc].rearrange("p kc o -> p (kc o)"))
                cP_up[oc] = P(uppan[oc % 2][:],
                              d_upt[oc].rearrange("p kc o -> p (kc o)"))
                E.wait("tensor", sP_h, cP_up[oc])
                if oc == 0:
                    E.wait("tensor", sV_h, c_scale)
                if oc >= 2:
                    E.wait("tensor", sS_h, cS_ev[oc - 2])
                for kc in range(32):
                    for hh in range(2):
                        T(lambda e, oc=oc, kc=kc, hh=hh, bp=bp: e.matmul(
                            bank(bp + hh),
                            wpan[oc % 2][:, kc * 128:(kc + 1) * 128],
                            ln[:, kc * 1024 + hh * 512:
                               kc * 1024 + hh * 512 + 512],
                            start=(kc == 0), stop=False))
                for kc4 in range(4):
                    for hh in range(2):
                        tv = T(lambda e, oc=oc, kc4=kc4, hh=hh, bp=bp:
                               e.matmul(
                                   bank(bp + hh),
                                   uppan[oc % 2][:, kc4 * 128:(kc4 + 1) * 128],
                                   hb[:, kc4 * 1024 + hh * 512:
                                      kc4 * 1024 + hh * 512 + 512],
                                   start=False, stop=(kc4 == 3)),
                               inc=(kc4 == 3 and hh == 1))
                        if tv is not None:
                            tgrp[oc] = tv
                E.wait("scalar", sT_h, tgrp[oc])
                if oc >= 2:
                    E.wait("scalar", sV_h, cV_up[oc - 2])
                cS_ev[oc] = S(lambda e, oc=oc, bp=bp: e.mul(
                    tmp[oc % 2][:], pt[:, bp * 512:(bp + 2) * 512], 0.5),
                    inc=True)
                if oc >= 2:
                    E.wait("sync", sY_h, cY_xw[oc - 2])
                E.wait("sync", sY_h, xw.get(oc) if xw else None)
                cy_r = Y(xbuf[oc % 2][:], d_x[oc])
                E.wait("vector", sY_h, cy_r)
                E.wait("vector", sS_h, cS_ev[oc])
                cV_up[oc] = V(lambda e, oc=oc: e.tensor_tensor(
                    xbuf[oc % 2][:], xbuf[oc % 2][:], tmp[oc % 2][:],
                    ALU.add), inc=True)
                E.wait("sync", sV_h, cV_up[oc])
                cY_xw[oc] = Y(d_x[oc], xbuf[oc % 2][:])
            return cY_xw

        if stage >= 1:
            st1 = ln_stats_from_ln()
            bc1 = ln_stats_tail(st1)

        if stage >= 2 or stage == 99:
            E.wait("tensor", sP_h, cP_w2)
            tpre = {}
            cV_ag = {}
            cY_ag = {}
            cY_msk = {}
            cpe_pre = {}
            for j in range(n_sc):
                if j >= 2:
                    E.wait("gpsimd", sT_h, tpre[j - 2])
                cP_w1 = P(w1pan[j % 2][:],
                          d_w1sl[j].rearrange("p dc s -> p (dc s)"))
                if j >= 2:
                    E.wait("scalar", sP_h, cV_ag[j - 2])
                cY_msk[j] = SD(mskb[j % 2], d_masksl[j])
                E.wait("tensor", sP_h, cP_w1)
                if j >= 2:
                    E.wait("tensor", sS_h, cpe_pre[j - 2])
                for dc in range(32):
                    tv = T(lambda e, j=j, dc=dc: e.matmul(
                        bank(j % 2), w1pan[j % 2][:, dc * 128:(dc + 1) * 128],
                        w2sb[:, dc * 512:(dc + 1) * 512],
                        start=(dc == 0), stop=(dc == 31)), inc=(dc == 31))
                    if tv is not None:
                        tpre[j] = tv
                E.wait("scalar", sT_h, tpre[j])
                if j >= 2:
                    E.wait("scalar", sP_h, cV_ag[j - 2])
                cpe_pre[j] = SC(lambda e, j=j: e.copy(petmp[j % 2][:],
                                                      bank(j % 2)))
                E.wait("gpsimd", sS_h, cpe_pre[j])
                E.wait("gpsimd", sSD_h, cY_msk[j])
                if j >= 2:
                    E.wait("gpsimd", sSD_h, cY_ag[j - 2])
                cV_ag[j] = E.op("gpsimd", lambda e, j=j: e.tensor_tensor(
                    agb[j % 2][:], petmp[j % 2][:], mskb[j % 2], ALU.mult),
                    sem=sP_h, name="sP")
                E.wait("scalar", sP_h, cV_ag[j])
                cY_ag[j] = SD(d_agin[j * 128:(j + 1) * 128, :], agb[j % 2][:])
            ag_wcount = cY_ag[n_sc - 1]
        else:
            ag_wcount = None

        if stage >= 1:
            ap1 = ln_apply(bc1, True, xw0)

        xwf = xw0
        if stage >= 1 or stage == 99:
            xw1 = ffn(1, xw0, ap1, ag_free=ag_wcount)
            xwf = xw1
        if stage >= 2 or stage == 99:
            E.wait("gpsimd", sSD_h, ag_wcount)
            BAR()
            E.op("gpsimd", lambda e: e.collective_compute(
                "AllGather", ALU.bypass, replica_groups=[list(range(NCORE))],
                ins=[d_agin[:]], outs=[d_agout[:]]), sem=sAG, name="sAG")
            E.wait("gpsimd", sAG, 1)
            BAR()
        if stage >= 2 and stage != 99:
            st2 = ln_stats_from_x(xw1)
            bc2 = ln_stats_tail(st2)
            ap2 = ln_apply(bc2, False, xw1)
        BAR()

        if stage < 2 or stage == 99:
            xw_attn = None
        if stage >= 2 and stage != 99:
         E.wait("tensor", sAG, 1)
         E.wait("sync", sAG, 1)
         ngrp = (n_sc + 7) // 8
         agr = d_agout.rearrange("(r j p) t -> r p j t", p=128, j=n_sc)
         cY_agl = {}
         tgrp = {}
         cS_gel = {}
         cY_axw = {}
         cVu_att = {}
         for r in range(NCORE):
             for g in range(ngrp):
                 seq = r * ngrp + g
                 ns_g = min(8, n_sc - g * 8)
                 if seq >= 2:
                     E.wait("sync", sT_h, tgrp[seq - 2])
                 cY_agl[seq] = Y(
                     wpan[seq % 2][:, 0:ns_g * 512].rearrange(
                         "p (j t) -> p j t", j=ns_g),
                     agr[r][:, g * 8:g * 8 + ns_g, :])
                 E.wait("tensor", sY_h, cY_agl[seq])
                 if g == 0 and r >= 1:
                     E.wait("tensor", sS_h, cS_gel[(r - 1, 7)])
                 for jj in range(ns_g):
                     j = g * 8 + jj
                     gkc = SCL[r][j]
                     for tcl in range(4):
                         for hh in range(2):
                             bk = tcl * 2 + hh
                             tv = T(lambda e, seq=seq, jj=jj, tcl=tcl, hh=hh,
                                    bk=bk, gkc=gkc, f=(j == 0),
                                    l=(j == n_sc - 1): e.matmul(
                                        bank(bk),
                                        wpan[seq % 2][:, jj * 512 + tcl * 128:
                                                      jj * 512 + tcl * 128 + 128],
                                        ln[:, gkc * 1024 + hh * 512:
                                           gkc * 1024 + hh * 512 + 512],
                                        start=f, stop=l),
                                    inc=(jj == ns_g - 1 and tcl == 3
                                         and hh == 1))
                             if tv is not None:
                                 tgrp[seq] = tv
             for tcl in range(4):
                 tc = r * 4 + tcl
                 par = tc % 2
                 gb = gbuf[tc % 4]
                 E.wait("scalar", sT_h, tgrp[r * ngrp + ngrp - 1])
                 E.wait("scalar", sV_h, cVu_att.get(tc - 4))
                 for hh in range(2):
                     cS_gel[(r, tcl * 2 + hh)] = S(
                         lambda e, tcl=tcl, hh=hh, tc=tc, gb=gb:
                         e.activation(
                             gb[:, hh * 512:hh * 512 + 512],
                             bank(tcl * 2 + hh), AF.Gelu,
                             bias=b2sb[:, tc:tc + 1]), inc=True)
                 seq2 = r * 4 + tcl
                 if seq2 >= 2:
                     E.wait("gpsimd", sY_h, cY_axw[seq2 - 2])
                 E.wait("gpsimd", sY_h, xw1[tc])
                 cy_r = P(xbuf[par][:], d_x[tc])
                 E.wait("vector", sP_h, cy_r)
                 E.wait("vector", sS_h, cS_gel[(r, tcl * 2 + 1)])
                 cvu = V(lambda e, par=par, gb=gb: e.tensor_tensor(
                     xbuf[par][:], xbuf[par][:], gb[:], ALU.add),
                     inc=True)
                 cVu_att[tc] = cvu
                 E.wait("sync", sV_h, cvu)
                 cY_axw[seq2] = Y(d_x[tc], xbuf[par][:])
         xw_attn = {r * 4 + tcl: cY_axw[r * 4 + tcl]
                    for r in range(NCORE) for tcl in range(4)}
         xwf = xw_attn
         if stage >= 3:
            st3 = ln_stats_from_x(xw_attn)
            bc3 = ln_stats_tail(st3)
            ap3 = ln_apply(bc3, True, xw_attn)

        if stage >= 3 and stage != 99:
            xw2 = ffn(2, xw_attn, ap3)
            xwf = xw2
        BAR()

        outr = d_out.rearrange("(tb p) f -> p tb f", p=128)
        cY_ow = {}
        cT_ot = {}
        cS_ob = {}
        for fc in range(32):
            par = fc % 2
            if fc >= 2:
                E.wait("gpsimd", sT_h, cT_ot[fc - 2])
            E.wait("gpsimd", sY_h, xwf[fc])
            cy = P(xbuf[par][:], d_x[fc])
            E.wait("tensor", sP_h, cy)
            E.wait("tensor", sS_h, cS_ob.get(fc - 1))
            for tb in range(8):
                tv = T(lambda e, par=par, tb=tb: e.transpose(
                    bank(tb, 0, 128), xbuf[par][:, tb * 128:(tb + 1) * 128],
                    idn_sb[:]), inc=(tb == 7))
                if tb == 7:
                    cT_ot[fc] = tv
            E.wait("scalar", sT_h, cT_ot[fc])
            E.wait("scalar", sY_h, cY_ow.get(fc - 2))
            sv = None
            for tb in range(8):
                sv = S(lambda e, par=par, tb=tb: e.copy(
                    tmp[par][:, tb * 128:(tb + 1) * 128], bank(tb, 0, 128)),
                    inc=(tb == 7))
            cS_ob[fc] = sv
            E.wait("sync", sS_h, cS_ob[fc])
            cY_ow[fc] = Y(outr[:, :, fc * 128:(fc + 1) * 128],
                          tmp[par].rearrange("p (tb t) -> p tb t", tb=8))
        if dbg:
            for fc in range(32):
                Y(d_dbg3[fc], ln[:, fc * 1024:(fc + 1) * 1024])
            Y(d_dbg2[:, 0:1024], asm[0][:])
            Y(d_dbg2[:, 1024:2048], asm[1][:])
            dv = V(lambda e: e.tensor_copy(armall[:, 120:124], coeffv[:, 0:4]),
                   inc=True)
            dv = V(lambda e: e.tensor_copy(armall[0:8, 124:125], ccolb[:]),
                   inc=True)
            dv = V(lambda e: e.tensor_copy(armall[0:1, 64:120],
                                           srow[0:1, 0:56]), inc=True)
            E.wait("sync", sV_h, dv)
            Y(d_dbg[:, 0:32], macc[:])
            Y(d_dbg[:, 32:64], pmsb[:])
            Y(d_dbg[:, 64:128], armall[:, 64:128])
        E.wait("sync", sY_h, E.cnt["sY"])
        E.wait("gpsimd", sP_h, E.cnt["sP"])
        BAR()

        build._last_E = E
        with nc.Block() as block:
            for eng_name in ENGS:
                def make(eng_name):
                    def body(e):
                        for fn in E.q[eng_name]:
                            fn(e)
                    return body
                getattr(block, eng_name)(make(eng_name))

    return nc


def run_staged(inputs, stage=3, dbg=False, want_dbg=False):
    in_maps, SCL, n_sc = _host_prep(inputs)
    nc = build(SCL, n_sc, stage=stage, dbg=dbg)
    res = run_bass_kernel_spmd(nc, in_maps, core_ids=list(range(NCORE)))
    out = np.empty((4, 2048, 4096), dtype=np.float32)
    for c in range(NCORE):
        b, h = c // 2, c % 2
        out[b, h * 1024:(h + 1) * 1024, :] = res.results[c]["out"]
    if want_dbg:
        return out, [res.results[c]["dbg"] for c in range(NCORE)], \
            [res.results[c]["dbg2"] for c in range(NCORE)], \
            [res.results[c]["dbg3"] for c in range(NCORE)]
    return out


def kernel(**inputs):
    return run_staged(inputs, stage=3)
